# version 08
# baseline (speedup 1.0000x reference)
import contextlib
import numpy as np
import concourse.bass as bass
import concourse.mybir as mybir
from concourse.bass_utils import run_bass_kernel_spmd

F32 = mybir.dt.float32
BF16 = mybir.dt.bfloat16
AF = mybir.ActivationFunctionType
ALU = mybir.AluOpType
AX = mybir.AxisListType

D = 1024
H = 16
DH = 64
DF = 4096
ST = 512
SC = 128
NSC = ST // SC

ENGINES = ("sync", "tensor", "vector", "scalar", "gpsimd")


class Sem:

    def __init__(self, h):
        self.h = h
        self.n = 0

    def inc(self, k=1):
        self.n += k
        return self.n


class Builder:
    def __init__(self, nc):
        self.nc = nc
        self.ops = []

    def em(self, eng, fn):
        self.ops.append((eng, fn))

    def flush(self):
        ops = self.ops
        self.ops = []
        with self.nc.Block() as blk:
            for name in ENGINES:
                mine = [f for e, f in ops if e == name]
                if not mine:
                    continue

                def body(eng, _mine=mine):
                    for f in _mine:
                        f(eng)
                getattr(blk, name)(body)


def build_program(L=8192, debug_taps=(), stop_phase=99, attn_super=99, attn_level=99):
    NST = L // ST
    HALF = L // 2
    NF = HALF // ST
    nc = bass.Bass("TRN2", target_bir_lowering=False, debug=False)

    def din(name, shape):
        return nc.dram_tensor(name, list(shape), F32, kind="ExternalInput").ap()

    x_d = din("x", (L, D))
    wq_d = din("q1_w", (D, D)); bq_d = din("q1_b", (D,))
    wk_d = din("k1_w", (D, D)); bk_d = din("k1_b", (D,))
    wk2_d = din("k2_w", (D, D)); bk2_d = din("k2_b", (D,))
    gw_d = din("gw_w", (DH, 1)); gb_d = din("gw_b", (1,))
    sw_d = din("sw_w", (DH, 1)); sb_d = din("sw_b", (1,))
    wcp_d = din("cp_w", (D, D)); bcp_d = din("cp_b", (D,))
    w1_d = din("w1_w", (DF, D)); s1_d = din("w1_s", (DF,))
    w2_d = din("w2_w", (DF, D)); s2_d = din("w2_s", (DF,))
    h1_d = din("h1_w", (DF, DF)); sh1_d = din("h1_s", (DF,))
    h2_d = din("h2_w", (DF, DF)); sh2_d = din("h2_s", (DF,))
    w3_d = din("w3_w", (D, DF)); s3_d = din("w3_s", (D,))
    hsel_d = din("hsel", (1, 2))

    out_d = nc.dram_tensor("out", [HALF, D], F32, kind="ExternalOutput").ap()

    xbf_d = nc.dram_tensor("xbf_scr", [L, D], BF16, kind="Internal").ap()
    xlo_d = nc.dram_tensor("xlo_scr", [L, D], BF16, kind="Internal").ap()
    x1f_d = nc.dram_tensor("x1f_scr", [L, D], F32, kind="Internal").ap()
    x1bf_d = nc.dram_tensor("x1bf_scr", [L, D], BF16, kind="Internal").ap()
    wn_d = {}
    wT_d = {}
    for nm, (o_dim, i_dim) in (("w1", (DF, D)), ("w2", (DF, D)),
                               ("h1", (DF, DF)), ("h2", (DF, DF)), ("w3", (D, DF))):
        wn_d[nm] = nc.dram_tensor(nm + "n_scr", [o_dim, i_dim], BF16, kind="Internal").ap()
        wT_d[nm] = nc.dram_tensor(nm + "T_scr", [i_dim, o_dim], BF16, kind="Internal").ap()
    dbg_x1 = None
    if "x1" in debug_taps:
        dbg_x1 = nc.dram_tensor("dbg_x1", [L, D], F32, kind="ExternalOutput").ap()

    es = contextlib.ExitStack()
    with es:
        def sbuf(name, shape, dt=F32):
            return es.enter_context(nc.sbuf_tensor(name, list(shape), dt))

        def sem(name):
            return Sem(es.enter_context(nc.semaphore(name)))

        B = Builder(nc)
        p = sem("p")
        v = sem("v")
        se = sem("se")
        d_ld = sem("d_ld")
        d_st = sem("d_st")
        d_out = sem("d_out")
        gp = sem("gp")
        dma_sems = []
        dbg_sems = [None]

        ident_bf = sbuf("ident_bf", (128, 128), BF16)
        mask_bf = sbuf("mask_bf", (128, 512), BF16)
        tri_f = sbuf("tri_f", (128, 128), F32)
        ones_col = sbuf("ones_col", (128, 1), F32)
        ones_row = sbuf("ones_row", (1, 128), F32)
        ones_row_bf = sbuf("ones_row_bf", (1, 128), BF16)
        rows_bf = sbuf("rows_bf", (128, D + 512), BF16)
        bq_rbf = rows_bf[0:1, 0:D]
        bk_rbf = rows_bf[32:33, 0:D]
        bk2_rbf = rows_bf[64:65, 0:D]
        bcp_bf = sbuf("bcp_bf", (1, D), BF16)
        bdg_hi = sbuf("bdg_hi", (128, 128), BF16)
        bdg_lo = sbuf("bdg_lo", (128, 128), BF16)
        bds_bf = sbuf("bds_bf", (128, 128), BF16)
        gwb_bc = sbuf("gwb_bc", (128, 1), F32)
        swb_bc = sbuf("swb_bc", (128, 1), F32)
        hsel_sb = sbuf("hsel_sb", (128, 2), F32)
        eps_col = sbuf("eps_col", (128, 1), F32)
        rd_col = sbuf("rd_col", (128, 1), F32)
        s1_sb = sbuf("s1_sb", (128, 32), F32)
        s2_sb = sbuf("s2_sb", (128, 32), F32)
        sh1_sb = sbuf("sh1_sb", (128, 32), F32)
        sh2_sb = sbuf("sh2_sb", (128, 32), F32)
        s3_sb = sbuf("s3_sb", (128, 8), F32)
        smallv = sbuf("smallv", (1, 4), F32)

        with contextlib.ExitStack() as e0:
            ps_tmp = e0.enter_context(nc.psum_tensor("ps_tmp", [128, 8], F32))
            io_p = e0.enter_context(nc.sbuf_tensor("io_p", [128, 128], F32))
            io_f = e0.enter_context(nc.sbuf_tensor("io_f", [128, 128], F32))
            bdg_f = e0.enter_context(nc.sbuf_tensor("bdg_f", [128, 128], F32))
            bds_f = e0.enter_context(nc.sbuf_tensor("bds_f", [128, 128], F32))
            bq_row = e0.enter_context(nc.sbuf_tensor("bq_row", [1, D], F32))
            bk_row = e0.enter_context(nc.sbuf_tensor("bk_row", [1, D], F32))
            bk2_row = e0.enter_context(nc.sbuf_tensor("bk2_row", [1, D], F32))
            bcp_f = e0.enter_context(nc.sbuf_tensor("bcp_f", [1, D], F32))

            small_jobs = []
            small_jobs.append((bq_row[:], bq_d.rearrange("(a b) -> a b", a=1)))
            small_jobs.append((bk_row[:], bk_d.rearrange("(a b) -> a b", a=1)))
            small_jobs.append((bk2_row[:], bk2_d.rearrange("(a b) -> a b", a=1)))
            for oo in range(32):
                small_jobs.append((s1_sb[:, oo:oo + 1], s1_d[128 * oo:128 * oo + 128].rearrange("(q o) -> q o", o=1)))
                small_jobs.append((s2_sb[:, oo:oo + 1], s2_d[128 * oo:128 * oo + 128].rearrange("(q o) -> q o", o=1)))
                small_jobs.append((sh1_sb[:, oo:oo + 1], sh1_d[128 * oo:128 * oo + 128].rearrange("(q o) -> q o", o=1)))
                small_jobs.append((sh2_sb[:, oo:oo + 1], sh2_d[128 * oo:128 * oo + 128].rearrange("(q o) -> q o", o=1)))
            for oo in range(8):
                small_jobs.append((s3_sb[:, oo:oo + 1], s3_d[128 * oo:128 * oo + 128].rearrange("(q o) -> q o", o=1)))
            small_jobs.append((hsel_sb[0:1, 0:2], hsel_d))
            small_jobs.append((bcp_f[:], bcp_d.rearrange("(a b) -> a b", a=1)))
            small_jobs.append((smallv[0:1, 0:1], gb_d.rearrange("(a b) -> a b", a=1)))
            small_jobs.append((smallv[0:1, 1:2], sb_d.rearrange("(a b) -> a b", a=1)))
            d_ld.inc(16 * len(small_jobs))

            def p0_sync(sync, jobs=tuple(small_jobs)):
                for dst, src in jobs:
                    sync.dma_start(out=dst, in_=src).then_inc(d_ld.h, 16)
            B.em("sync", p0_sync)

            gp.inc(2)

            def p0_gp(gpsimd):
                gpsimd.iota(io_p[:], [[0, 128]], channel_multiplier=1,
                            allow_small_or_imprecise_dtypes=True)
                gpsimd.iota(io_f[:], [[1, 128]], channel_multiplier=0,
                            allow_small_or_imprecise_dtypes=True).then_inc(gp.h, 1)
                gpsimd.memset(bdg_f[:], 0.0)
                gpsimd.memset(bds_f[:], 0.0).then_inc(gp.h, 1)
            B.em("gpsimd", p0_gp)

            tok_iota, tok_small = gp.n, d_ld.n
            v.inc()

            def p0_v(vector, ti=tok_iota, ts=tok_small):
                vector.wait_ge(gp.h, ti)
                vector.wait_ge(d_ld.h, ts)
                vector.tensor_tensor(tri_f[:], io_p[:], io_f[:], ALU.is_le)
                for g in range(4):
                    vector.tensor_copy(mask_bf[:, 128 * g:128 * g + 128], tri_f[:])
                vector.tensor_tensor(io_p[:], io_p[:], io_f[:], ALU.is_equal)
                vector.tensor_copy(ident_bf[:], io_p[:])
                vector.memset(ones_col[:], 1.0)
                vector.memset(eps_col[:], 1e-5)
                vector.memset(rd_col[:], 1.0 / D)
                vector.tensor_copy(ones_row[:], tri_f[0:1, :])
                vector.tensor_copy(ones_row_bf[:], tri_f[0:1, :])
                vector.tensor_copy(bcp_bf[:], bcp_f[:])
                vector.tensor_copy(bq_rbf, bq_row[:])
                vector.tensor_copy(bk_rbf, bk_row[:])
                vector.tensor_copy(bk2_rbf, bk2_row[:])
                vector.memset(rows_bf[0:1, D:D + 512], 1.0)
                vector.memset(rows_bf[32:33, D:D + 512], 1.0)
                vector.memset(rows_bf[64:65, D:D + 512], 1.0).then_inc(v.h, 1)
            B.em("vector", p0_v)

            tok_vmemset = v.n
            d_ld.inc(16 * 32)

            def p0_bd(sync, tok=tok_vmemset):
                sync.wait_ge(v.h, tok)
                for j in range(8):
                    for ht in range(2):
                        c0 = 18 * j + ht
                        sync.dma_start(out=bdg_f[64 * ht:64 * ht + 64, c0:c0 + 1],
                                       in_=gw_d).then_inc(d_ld.h, 16)
                        sync.dma_start(out=bds_f[64 * ht:64 * ht + 64, c0:c0 + 1],
                                       in_=sw_d).then_inc(d_ld.h, 16)
            B.em("sync", p0_bd)

            tok_bd = d_ld.n
            v.inc()

            def p0_v2(vector, tok=tok_bd):
                vector.wait_ge(d_ld.h, tok)
                vector.tensor_copy(bdg_hi[:], bdg_f[:])
                vector.tensor_copy(bds_bf[:], bds_f[:])
                vector.tensor_tensor(io_f[:], bdg_f[:], bdg_hi[:], ALU.subtract)
                vector.tensor_copy(bdg_lo[:], io_f[:]).then_inc(v.h, 1)
            B.em("vector", p0_v2)

            tok_v2 = v.n
            p.inc()

            def p0_t(tensor, tok=tok_v2):
                tensor.wait_ge(v.h, tok)
                tensor.matmul(ps_tmp[:, 0:2], ones_row[:], smallv[0:1, 0:2],
                              start=True, stop=True)
                tensor.matmul(ps_tmp[:, 4:6], ones_row[:], hsel_sb[0:1, 0:2],
                              start=True, stop=True).then_inc(p.h, 1)
            B.em("tensor", p0_t)

            tok_bc = p.n
            v.inc()

            def p0_v3(vector, tok=tok_bc):
                vector.wait_ge(p.h, tok)
                vector.tensor_copy(gwb_bc[:], ps_tmp[:, 0:1])
                vector.tensor_copy(swb_bc[:], ps_tmp[:, 1:2])
                vector.tensor_copy(hsel_sb[:, 0:2], ps_tmp[:, 4:6]).then_inc(v.h, 1)
            B.em("vector", p0_v3)
            B.flush()
        if stop_phase <= 0:
            return nc

        with contextlib.ExitStack() as esW:
            wq_sb = esW.enter_context(nc.sbuf_tensor("wq_sb", [128, 8 * D], BF16))
            wk_sb = esW.enter_context(nc.sbuf_tensor("wk_sb", [128, 8 * D], BF16))
            wk2_sb = esW.enter_context(nc.sbuf_tensor("wk2_sb", [128, 8 * D], BF16))
            wcp_sb = esW.enter_context(nc.sbuf_tensor("wcp_sb", [128, 8 * D], BF16))

            with contextlib.ExitStack() as e1:
                wstage = [e1.enter_context(nc.sbuf_tensor(f"wstage{i}", [128, D], F32))
                          for i in range(2)]
                jobs = []
                for wdram, wsb in ((wq_d, wq_sb), (wk_d, wk_sb), (wk2_d, wk2_sb), (wcp_d, wcp_sb)):
                    for k in range(8):
                        jobs.append((wdram[128 * k:128 * k + 128, :], wsb, D * k))
                conv_tok = []
                for i, (src, dst, c0) in enumerate(jobs):
                    gate = conv_tok[i - 2] if i >= 2 else None
                    tok = d_ld.inc(16)

                    def lj(sync, src=src, i=i, gate=gate):
                        if gate is not None:
                            sync.wait_ge(v.h, gate)
                        sync.dma_start(out=wstage[i % 2][:], in_=src).then_inc(d_ld.h, 16)
                    B.em("sync", lj)
                    v.inc()

                    def cj(vector, dst=dst, c0=c0, i=i, tok=tok):
                        vector.wait_ge(d_ld.h, tok)
                        vector.tensor_copy(dst[:, c0:c0 + D], wstage[i % 2][:]).then_inc(v.h, 1)
                    B.em("vector", cj)
                    conv_tok.append(v.n)
                B.flush()
            if stop_phase <= 1:
                return nc

            with contextlib.ExitStack() as e2:
                xs_f = [e2.enter_context(nc.sbuf_tensor(f"xs_f{i}", [128, D], F32)) for i in range(2)]
                xs_hi = [e2.enter_context(nc.sbuf_tensor(f"xs_hi{i}", [128, D], BF16)) for i in range(2)]
                xs_lo = [e2.enter_context(nc.sbuf_tensor(f"xs_lo{i}", [128, D], BF16)) for i in range(2)]
                xs_lf = [e2.enter_context(nc.sbuf_tensor(f"xs_lf{i}", [128, D], F32)) for i in range(2)]
                fst = [e2.enter_context(nc.sbuf_tensor(f"fst{i}", [128, DF], F32)) for i in range(2)]
                fbf = [e2.enter_context(nc.sbuf_tensor(f"fbf{i}", [128, DF], BF16)) for i in range(2)]
                tbuf = [e2.enter_context(nc.sbuf_tensor(f"tbuf{i}", [128, DF], BF16)) for i in range(2)]

                NT = L // 128
                store_tok = []
                for i in range(NT):
                    gate = store_tok[i - 2] if i >= 2 else None
                    tok = d_ld.inc(16)

                    def xl(sync, i=i, gate=gate):
                        if gate is not None:
                            sync.wait_ge(d_st.h, gate)
                        sync.dma_start(out=xs_f[i % 2][:],
                                       in_=x_d[128 * i:128 * i + 128, :]).then_inc(d_ld.h, 16)
                    B.em("sync", xl)
                    v.inc()

                    def xc(vector, i=i, tok=tok):
                        vector.wait_ge(d_ld.h, tok)
                        b = i % 2
                        vector.tensor_copy(xs_hi[b][:], xs_f[b][:])
                        vector.tensor_tensor(xs_lf[b][:], xs_f[b][:], xs_hi[b][:], ALU.subtract)
                        vector.tensor_copy(xs_lo[b][:], xs_lf[b][:]).then_inc(v.h, 1)
                    B.em("vector", xc)
                    vtok = v.n
                    d_st.inc(32)

                    def xo(gpsimd, i=i, tok=vtok):
                        gpsimd.wait_ge(v.h, tok)
                        b = i % 2
                        gpsimd.dma_start(out=xbf_d[128 * i:128 * i + 128, :],
                                         in_=xs_hi[b][:]).then_inc(d_st.h, 16)
                        gpsimd.dma_start(out=xlo_d[128 * i:128 * i + 128, :],
                                         in_=xs_lo[b][:]).then_inc(d_st.h, 16)
                    B.em("gpsimd", xo)
                    store_tok.append(d_st.n)

                a_store_tok = []
                ai = 0
                b_store_tok = []
                bi = 0
                for nm, src_dram, width, n_t, o_dim, i_dim in (
                    ("w1", w1_d, D, 32, DF, D), ("w2", w2_d, D, 32, DF, D),
                    ("h1", h1_d, DF, 32, DF, DF), ("h2", h2_d, DF, 32, DF, DF),
                    ("w3", w3_d, DF, 8, D, DF),
                ):
                    for t in range(n_t):
                        scale = s3_sb[:, t:t + 1] if nm == "w3" else None
                        gate = a_store_tok[ai - 2] if ai >= 2 else None
                        tok = d_ld.inc(16)

                        def fl(sync, nm=nm, src=src_dram[128 * t:128 * t + 128, :],
                               w=width, gate=gate, b=ai % 2):
                            if gate is not None:
                                sync.wait_ge(d_st.h, gate)
                            sync.dma_start(out=fst[b][:, 0:w], in_=src).then_inc(d_ld.h, 16)
                        B.em("sync", fl)
                        v.inc()

                        def fc(vector, w=width, scale=scale, b=ai % 2, tok=tok):
                            vector.wait_ge(d_ld.h, tok)
                            if scale is None:
                                vector.tensor_copy(fbf[b][:, 0:w], fst[b][:, 0:w]).then_inc(v.h, 1)
                            else:
                                vector.tensor_scalar_mul(fbf[b][:, 0:w], fst[b][:, 0:w],
                                                         scale).then_inc(v.h, 1)
                        B.em("vector", fc)
                        vtok = v.n
                        d_st.inc(16)

                        def fo(gpsimd, dst=wn_d[nm][128 * t:128 * t + 128, :],
                               w=width, b=ai % 2, tok=vtok):
                            gpsimd.wait_ge(v.h, tok)
                            gpsimd.dma_start(out=dst, in_=fbf[b][:, 0:w]).then_inc(d_st.h, 16)
                        B.em("gpsimd", fo)
                        a_store_tok.append(d_st.n)
                        ai += 1
                    w_done = d_st.n
                    for it in range(i_dim // 128):
                        gate = b_store_tok[bi - 2] if bi >= 2 else None
                        tok = d_ld.inc(16)

                        def tl(sync, src=wn_d[nm][0:o_dim, 128 * it:128 * it + 128],
                               o_dim=o_dim, gate=gate, b=bi % 2, w_done=w_done):
                            sync.wait_ge(d_st.h, w_done if gate is None else max(gate, w_done))
                            sync.dma_start(out=tbuf[b][:, 0:o_dim], in_=src,
                                           transpose=True).then_inc(d_ld.h, 16)
                        B.em("sync", tl)
                        d_st.inc(16)

                        def to(gpsimd, dst=wT_d[nm][128 * it:128 * it + 128, 0:o_dim],
                               o_dim=o_dim, b=bi % 2, tok=tok):
                            gpsimd.wait_ge(d_ld.h, tok)
                            gpsimd.dma_start(out=dst, in_=tbuf[b][:, 0:o_dim]).then_inc(d_st.h, 16)
                        B.em("gpsimd", to)
                        b_store_tok.append(d_st.n)
                        bi += 1
                B.flush()
            if stop_phase <= 4:
                return nc

            with contextlib.ExitStack() as eA:
                def sbufA(name, shape, dt=F32):
                    return eA.enter_context(nc.sbuf_tensor(name, list(shape), dt))

                xn2 = [sbufA(f"xn2_{i}", (128, D)) for i in range(2)]
                xsh = [sbufA(f"xsh{i}", (128, D)) for i in range(2)]
                xT_hi = sbufA("xT_hi", (128, 8 * ST), BF16)
                xT_lo = sbufA("xT_lo", (128, 8 * ST), BF16)
                qTz = [sbufA(f"qTz{i}", (128, 8 * ST), BF16) for i in range(2)]
                kT = sbufA("kT", (128, 8 * ST), BF16)
                k2T = sbufA("k2T", (128, 8 * ST), BF16)
                q2Tz = [sbufA(f"q2Tz{i}", (128, 8 * ST), BF16) for i in range(2)]
                g_sb = sbufA("g_sb", (128, 16 * NSC))
                logg = sbufA("logg", (128, 16 * NSC))
                gc_sb = sbufA("gc_sb", (128, 16 * NSC))
                rinv = sbufA("rinv", (128, 16 * NSC))
                r_sb = sbufA("r_sb", (128, 16 * NSC))
                carry = sbufA("carry", (1, 16), F32)
                s1bz = sbufA("s1bz", (128, 16 * DH), BF16)
                s2bz = sbufA("s2bz", (128, 16 * DH), BF16)
                s1f = sbufA("s1f", (128, 512))
                s2f = sbufA("s2f", (128, 512))
                vt_bf = sbufA("vt_bf", (128, D), BF16)
                b_bf = sbufA("b_bf", (128, 16 * SC), BF16)
                o1f = sbufA("o1f", (128, D))
                o1bf = sbufA("o1bf", (128, D), BF16)
                e_bf = sbufA("e_bf", (128, D), BF16)
                knat = sbufA("knat", (128, D), BF16)
                k2nat = sbufA("k2nat", (128, D), BF16)
                W2 = ST + 2
                oT = sbufA("oT", (128, 8 * W2), BF16)
                o2carry = sbufA("o2carry", (128, 8), BF16)
                z1 = sbufA("z1", (128, D))
                stat = sbufA("stat", (128, 8))
                x1bf_sb = sbufA("x1bf_sb", (128, D), BF16)

                ps_a = eA.enter_context(nc.psum_tensor("ps_a", [128, 1024], F32))
                ps_o = eA.enter_context(nc.psum_tensor("ps_o", [128, 1024], F32))
                ps_t1 = eA.enter_context(nc.psum_tensor("ps_t1", [128, 1024], BF16))
                ps_t2 = eA.enter_context(nc.psum_tensor("ps_t2", [128, 1024], BF16))
                ps_d1 = eA.enter_context(nc.psum_tensor("ps_d1", [128, 512], F32))
                ps_d2 = eA.enter_context(nc.psum_tensor("ps_d2", [128, 512], F32))

                d_xn = sem("d_xn")
                d_xT = sem("d_xT")
                d_xsh = sem("d_xsh")
                d_sp = sem("d_sp")
                dma_sems.extend([d_xn, d_xT, d_xsh, d_sp])

                v.inc()

                def a_init(vector):
                    vector.memset(carry[:], 0.0)
                    vector.memset(s1bz[:], 0.0)
                    vector.memset(s2bz[:], 0.0)
                    vector.memset(s1f[:], 0.0)
                    vector.memset(s2f[:], 0.0)
                    vector.memset(qTz[0][64:128, :], 0.0)
                    vector.memset(qTz[1][0:64, :], 0.0)
                    vector.memset(q2Tz[0][64:128, :], 0.0)
                    vector.memset(q2Tz[1][0:64, :], 0.0)
                    vector.memset(o2carry[:], 0.0).then_inc(v.h, 1)
                B.em("vector", a_init)

                xn_tok = {}
                xT_tok = {}
                xlo_tok = {}
                xsh_tok = {}

                dump_d = {}
                d_dbg = sem("d_dbg") if "deep" in debug_taps else None
                dbg_sems[0] = d_dbg

                def emit_dump(nm, ap_, shape, dt):
                    if "deep" not in debug_taps:
                        return
                    t = nc.dram_tensor("dbg_" + nm, list(shape), dt,
                                       kind="ExternalOutput").ap()
                    dump_d[nm] = t
                    gate = v.n
                    gate_s = se.n
                    gate_p = p.n
                    d_dbg.inc(16)

                    def f(gpsimd, t=t, ap_=ap_, gate=gate, gate_s=gate_s, gate_p=gate_p):
                        gpsimd.wait_ge(v.h, gate)
                        gpsimd.wait_ge(se.h, gate_s)
                        gpsimd.wait_ge(p.h, gate_p)
                        gpsimd.dma_start(out=t, in_=ap_).then_inc(d_dbg.h, 16)
                    B.em("gpsimd", f)
                    tokd = d_dbg.n

                    def fw(vector, tokd=tokd):
                        vector.wait_ge(d_dbg.h, tokd)
                    B.em("vector", fw)

                def emit_load_xn(k):
                    if k >= NST * NSC:
                        return
                    t0 = SC * k
                    gate_v = v.n
                    gate_st = d_st.n
                    d_xn.inc(16)

                    def lx(sync, k=k, t0=t0, gate_v=gate_v, gate_st=gate_st):
                        sync.wait_ge(v.h, gate_v)
                        sync.wait_ge(d_st.h, gate_st)
                        sync.dma_start(out=xn2[k % 2][:],
                                       in_=x_d[t0:t0 + SC, :]).then_inc(d_xn.h, 16)
                    B.em("sync", lx)
                    xn_tok[k] = d_xn.n

                def emit_loads_xT(s):
                    if s >= NST:
                        return
                    t0 = s * ST
                    gate_p = p.n
                    gate_st = d_st.n
                    d_xT.inc(16 * 8)

                    def lxt(sync, s=s, t0=t0, gate_p=gate_p, gate_st=gate_st):
                        sync.wait_ge(p.h, gate_p)
                        sync.wait_ge(d_st.h, gate_st)
                        for j in range(8):
                            sync.dma_start(out=xT_hi[:, ST * j:ST * j + ST],
                                           in_=xbf_d[t0:t0 + ST, 128 * j:128 * j + 128],
                                           transpose=True).then_inc(d_xT.h, 16)
                    B.em("sync", lxt)
                    xT_tok[s] = d_xT.n

                def emit_loads_lo(s):
                    if s >= NST:
                        return
                    t0 = s * ST
                    gate_p, gate_v = p.n, v.n
                    d_xT.inc(16 * 8)

                    def lxl(sync, s=s, t0=t0, gate_p=gate_p, gate_v=gate_v, gate_st=d_st.n):
                        sync.wait_ge(p.h, gate_p)
                        sync.wait_ge(v.h, gate_v)
                        sync.wait_ge(d_st.h, gate_st)
                        for j in range(8):
                            sync.dma_start(out=xT_lo[:, ST * j:ST * j + ST],
                                           in_=xlo_d[t0:t0 + ST, 128 * j:128 * j + 128],
                                           transpose=True).then_inc(d_xT.h, 16)
                    B.em("sync", lxl)
                    xlo_tok[s] = d_xT.n

                def emit_load_xsh(k):
                    if k >= NST * NSC:
                        return
                    t0 = SC * k + 1
                    nrow = min(SC, L - t0)
                    gate_v = v.n
                    d_xsh.inc(16)

                    def lsh(sync, k=k, t0=t0, nrow=nrow, gate_v=gate_v):
                        sync.wait_ge(v.h, gate_v)
                        sync.dma_start(out=xsh[k % 2][0:nrow, :],
                                       in_=x_d[t0:t0 + nrow, :]).then_inc(d_xsh.h, 16)
                    B.em("sync", lsh)
                    xsh_tok[k] = d_xsh.n

                emit_loads_xT(0)
                emit_loads_lo(0)
                emit_load_xsh(0)
                emit_load_xsh(1)
                emit_load_xn(0)
                emit_load_xn(1)

                for s in range(min(NST, attn_super)):
                    xtb = xT_hi

                    projs = [(wq_sb, None, bq_rbf, rows_bf[0:1, D:D + 512], "q"),
                             (wk_sb, kT, bk_rbf, rows_bf[32:33, D:D + 512], "copy"),
                             (wk2_sb, k2T, bk2_rbf, rows_bf[64:65, D:D + 512], "sig")]
                    evac_tok = []
                    for pi, (wsb, dst, bias, onesr, kind) in enumerate(projs):
                        for j in range(8):
                            idx = pi * 8 + j
                            gate = evac_tok[idx - 2] if idx >= 2 else None
                            ptok = p.inc()

                            def pj_mm(tensor, wsb=wsb, bias=bias, onesr=onesr, j=j,
                                      idx=idx, gate=gate,
                                      xtb=xtb, tokT=xT_tok[s],
                                      gate_v=(v.n if idx == 0 else None)):
                                if idx == 0:
                                    tensor.wait_ge(d_xT.h, tokT)
                                    tensor.wait_ge(v.h, gate_v)
                                if gate is not None:
                                    tensor.wait_ge(se.h, gate)
                                reg = ps_o[:, 512 * (idx % 2):512 * (idx % 2) + 512]
                                for k in range(8):
                                    tensor.matmul(
                                        reg, wsb[:, D * k + 128 * j:D * k + 128 * j + 128],
                                        xtb[:, ST * k:ST * k + ST],
                                        start=(k == 0), stop=False)
                                mm = tensor.matmul(reg, bias[:, 128 * j:128 * j + 128],
                                                   onesr,
                                                   start=False, stop=True)
                                mm.then_inc(p.h, 1)
                            B.em("tensor", pj_mm)
                            se.inc()

                            def pj_ev(scalar, dst=dst, kind=kind, j=j, idx=idx,
                                      tok=ptok):
                                scalar.wait_ge(p.h, tok)
                                reg = ps_o[:, 512 * (idx % 2):512 * (idx % 2) + 512]
                                if kind == "q":
                                    scalar.copy(qTz[0][0:64, ST * j:ST * j + ST],
                                                reg[0:64, :])
                                    scalar.copy(qTz[1][64:128, ST * j:ST * j + ST],
                                                reg[64:128, :]).then_inc(se.h, 1)
                                elif kind == "copy":
                                    scalar.copy(dst[:, ST * j:ST * j + ST], reg).then_inc(se.h, 1)
                                else:
                                    scalar.activation(dst[:, ST * j:ST * j + ST], reg,
                                                      AF.Sigmoid,
                                                      scale=0.02 / 32.0).then_inc(se.h, 1)
                            B.em("scalar", pj_ev)
                            evac_tok.append(se.n)

                    v.inc()

                    def q2_v(vector, tok=evac_tok[7]):
                        vector.wait_ge(se.h, tok)
                        for j in range(8):
                            sl = slice(ST * j, ST * j + ST)
                            vector.tensor_scalar_mul(q2Tz[0][0:64, sl],
                                                     qTz[0][0:64, sl], 0.125)
                            vector.scalar_tensor_tensor(
                                q2Tz[0][0:64, sl], qTz[0][0:64, sl],
                                0.0025, q2Tz[0][0:64, sl], ALU.mult, ALU.min)
                            vector.tensor_scalar_mul(q2Tz[1][64:128, sl],
                                                     qTz[1][64:128, sl], 0.125)
                            ins = vector.scalar_tensor_tensor(
                                q2Tz[1][64:128, sl], qTz[1][64:128, sl],
                                0.0025, q2Tz[1][64:128, sl], ALU.mult, ALU.min)
                        ins.then_inc(v.h, 1)
                    B.em("vector", q2_v)

                    if s == 0:
                        emit_dump("qT", qTz[0][:], (128, 8 * ST), BF16)
                        emit_dump("kT", kT[:], (128, 8 * ST), BF16)
                        emit_dump("k2T", k2T[:], (128, 8 * ST), BF16)
                        emit_dump("q2T", q2Tz[0][:], (128, 8 * ST), BF16)

                    ptok_gr = p.inc()

                    def gr_mm(tensor, xtb=xtb, tok_ev=evac_tok[23],
                              tokTlo=xlo_tok[s], gate_v=v.n):
                        tensor.wait_ge(se.h, tok_ev)
                        tensor.wait_ge(d_xT.h, tokTlo)
                        tensor.wait_ge(v.h, gate_v)
                        for c in range(NSC):
                            reg = ps_a[:, 16 * c:16 * c + 16]
                            for j in range(8):
                                ca = slice(ST * j + SC * c, ST * j + SC * c + SC)
                                tensor.matmul(reg, xtb[:, ca], bdg_hi[:, 16 * j:16 * j + 16],
                                              start=(j == 0), stop=False)
                                tensor.matmul(reg, xtb[:, ca], bdg_lo[:, 16 * j:16 * j + 16],
                                              start=False, stop=False)
                                tensor.matmul(reg, xT_lo[:, ca], bdg_hi[:, 16 * j:16 * j + 16],
                                              start=False, stop=(j == 7))
                            regr = ps_a[:, 64 + 16 * c:64 + 16 * c + 16]
                            for j in range(8):
                                ca = slice(ST * j + SC * c, ST * j + SC * c + SC)
                                mm = tensor.matmul(regr, kT[:, ca], bds_bf[:, 16 * j:16 * j + 16],
                                                   start=(j == 0), stop=(j == 7))
                        mm.then_inc(p.h, 1)
                    B.em("tensor", gr_mm)

                    emit_loads_xT(s + 1)

                    se.inc()

                    def gr_act(scalar, tok=ptok_gr):
                        scalar.wait_ge(p.h, tok)
                        for c in range(NSC):
                            scalar.activation(g_sb[:, 16 * c:16 * c + 16],
                                              ps_a[:, 16 * c:16 * c + 16],
                                              AF.Sigmoid, bias=gwb_bc[:, 0:1])
                        for c in range(NSC):
                            ins = scalar.activation(r_sb[:, 16 * c:16 * c + 16],
                                                    ps_a[:, 64 + 16 * c:64 + 16 * c + 16],
                                                    AF.Silu, bias=swb_bc[:, 0:1])
                        ins.then_inc(se.h, 1)
                    B.em("scalar", gr_act)
                    tok_gract = se.n

                    v.inc()

                    def g_clip(vector, tok=tok_gract):
                        vector.wait_ge(se.h, tok)
                        vector.tensor_scalar_max(g_sb[:], g_sb[:], 1e-6).then_inc(v.h, 1)
                    B.em("vector", g_clip)
                    tok_gclip = v.n

                    se.inc()

                    def g_ln(scalar, tok=tok_gclip):
                        scalar.wait_ge(v.h, tok)
                        scalar.activation(logg[:], g_sb[:], AF.Ln).then_inc(se.h, 1)
                    B.em("scalar", g_ln)
                    tok_gln = se.n

                    for c in range(NSC):
                        ptok_cs = p.inc()

                        def cs_mm(tensor, s=s, c=c, tok=tok_gln, gate_v=v.n):
                            tensor.wait_ge(se.h, tok)
                            tensor.wait_ge(v.h, gate_v)
                            reg = ps_o[:, 16 * c:16 * c + 16]
                            first_glob = (s == 0 and c == 0)
                            tensor.matmul(reg, tri_f[:], logg[:, 16 * c:16 * c + 16],
                                          start=True, stop=first_glob)
                            if not first_glob:
                                tensor.matmul(reg, ones_row[:], carry[:],
                                              start=False, stop=True)
                            tensor.matmul(ps_o[0:1, 256 + 16 * c:256 + 16 * c + 16],
                                          ones_col[:], logg[:, 16 * c:16 * c + 16],
                                          start=True, stop=True).then_inc(p.h, 1)
                        B.em("tensor", cs_mm)
                        v.inc()

                        def cs_v(vector, s=s, c=c, tok=ptok_cs):
                            vector.wait_ge(p.h, tok)
                            vector.tensor_scalar(gc_sb[:, 16 * c:16 * c + 16],
                                                 ps_o[:, 16 * c:16 * c + 16],
                                                 -30.0, 30.0, ALU.max, ALU.min)
                            if s == 0 and c == 0:
                                vector.tensor_copy(carry[:],
                                                   ps_o[0:1, 256 + 16 * c:256 + 16 * c + 16]).then_inc(v.h, 1)
                            else:
                                vector.tensor_add(carry[:], carry[:],
                                                  ps_o[0:1, 256 + 16 * c:256 + 16 * c + 16]).then_inc(v.h, 1)
                        B.em("vector", cs_v)

                    tok_cs = v.n
                    se.inc()

                    def gc_exp(scalar, tok=tok_cs):
                        scalar.wait_ge(v.h, tok)
                        scalar.activation(gc_sb[:], gc_sb[:], AF.Exp).then_inc(se.h, 1)
                    B.em("scalar", gc_exp)
                    tok_exp = se.n

                    v.inc()

                    def rinv_v(vector, tok=tok_exp):
                        vector.wait_ge(se.h, tok)
                        vector.tensor_scalar_add(gc_sb[:], gc_sb[:], 1e-6).then_inc(v.h, 1)
                    B.em("vector", rinv_v)
                    gp.inc()

                    def rinv_g(gpsimd, tok=v.n):
                        gpsimd.wait_ge(v.h, tok)
                        gpsimd.tensor_copy(logg[:], gc_sb[:]).then_inc(gp.h, 1)
                    B.em("gpsimd", rinv_g)
                    v.inc()

                    def rinv_v2(vector, tok=gp.n):
                        vector.wait_ge(gp.h, tok)
                        vector.reciprocal(rinv[:], logg[:]).then_inc(v.h, 1)
                    B.em("vector", rinv_v2)
                    if s == 0:
                        emit_dump("loggc", logg[:], (128, 16 * NSC), F32)
                        emit_dump("recipv", rinv[:], (128, 16 * NSC), F32)
                    gp.inc()

                    def rinv_g2(gpsimd, tok=v.n):
                        gpsimd.wait_ge(v.h, tok)
                        gpsimd.tensor_mul(rinv[:], rinv[:], r_sb[:]).then_inc(gp.h, 1)
                    B.em("gpsimd", rinv_g2)

                    if s == 0:
                        emit_dump("gsb", g_sb[:], (128, 16 * NSC), F32)
                        emit_dump("gc", gc_sb[:], (128, 16 * NSC), F32)
                        emit_dump("rinv", rinv[:], (128, 16 * NSC), F32)
                        emit_dump("rsb", r_sb[:], (128, 16 * NSC), F32)

                    for c in range(NSC):
                        k_glob = s * NSC + c

                        if attn_level < 1:
                            continue
                        vtok_pre = v.n
                        vtokA = v.inc()

                        def stA(vector, s=s, c=c, k=k_glob,
                                tok_xn=xn_tok[k_glob], tok_g=gp.n):
                            vector.wait_ge(d_xn.h, tok_xn)
                            vector.wait_ge(gp.h, tok_g)
                            for h in range(16):
                                ins = vector.tensor_scalar_mul(
                                    vt_bf[:, DH * h:DH * h + DH],
                                    xn2[k % 2][:, DH * h:DH * h + DH],
                                    rinv[:, 16 * c + h:16 * c + h + 1])
                            ins.then_inc(v.h, 1)
                        B.em("vector", stA)

                        ptok_trK = p.inc()

                        def trK(tensor, s=s, c=c, tok=vtok_pre):
                            tensor.wait_ge(v.h, tok)
                            for j in range(8):
                                tensor.transpose(ps_t2[:, 128 * j:128 * j + 128],
                                                 kT[:, ST * j + SC * c:ST * j + SC * c + SC],
                                                 ident_bf[:])
                            for j in range(8):
                                mm = tensor.transpose(ps_t1[:, 128 * j:128 * j + 128],
                                                      k2T[:, ST * j + SC * c:ST * j + SC * c + SC],
                                                      ident_bf[:])
                            mm.then_inc(p.h, 1)
                        B.em("tensor", trK)

                        v.inc()

                        def trK_ev(vector, tok=ptok_trK):
                            vector.wait_ge(p.h, tok)
                            vector.tensor_copy(knat[:], ps_t2[:])
                            vector.tensor_copy(k2nat[:], ps_t1[:]).then_inc(v.h, 1)
                        B.em("vector", trK_ev)


                        if attn_level < 2:
                            continue
                        for rnd in range(2):
                            ptok = p.inc()

                            def a1_mm(tensor, s=s, c=c, rnd=rnd,
                                      tok_v=v.n if rnd else vtok_pre):
                                tensor.wait_ge(v.h, tok_v)
                                for gg in range(2):
                                    g = 2 * rnd + gg
                                    for hh in range(4):
                                        h = 4 * g + hh
                                        j = h // 2
                                        sl = slice(ST * j + SC * c, ST * j + SC * c + SC)
                                        mm = tensor.matmul(
                                            ps_a[:, 512 * gg + SC * hh:512 * gg + SC * hh + SC],
                                            kT[:, sl], qTz[h % 2][:, sl],
                                            start=True, stop=True)
                                mm.then_inc(p.h, 1)
                            B.em("tensor", a1_mm)
                            v.inc()

                            if attn_level >= 2.5:
                                def a1_mask(vector, rnd=rnd, tok=ptok):
                                    vector.wait_ge(p.h, tok)
                                    for gg in range(2):
                                        g = 2 * rnd + gg
                                        ins = vector.tensor_tensor(
                                            b_bf[:, 4 * SC * g:4 * SC * g + 4 * SC],
                                            ps_a[:, 512 * gg:512 * gg + 512],
                                            mask_bf[:], ALU.mult)
                                    ins.then_inc(v.h, 1)
                                B.em("vector", a1_mask)
                            else:
                                def a1_maskless(vector, rnd=rnd, tok=ptok):
                                    vector.wait_ge(p.h, tok)
                                    ins = vector.memset(b_bf[:, 0:64], 0.0)
                                    ins.then_inc(v.h, 1)
                                B.em("vector", a1_maskless)

                        if attn_level < 3:
                            continue
                        ptok_o1 = p.inc()

                        def o1_mm(tensor, s=s, c=c, tok=v.n, tok_g=gp.n):
                            tensor.wait_ge(v.h, tok)
                            tensor.wait_ge(gp.h, tok_g)
                            for h in range(16):
                                j = h // 2
                                bank = ps_o[:, 512 * (h // 8) + DH * (h % 8):
                                            512 * (h // 8) + DH * (h % 8) + DH]
                                tensor.matmul(bank, b_bf[:, SC * h:SC * h + SC],
                                              vt_bf[:, DH * h:DH * h + DH],
                                              start=True, stop=False)
                                mm = tensor.matmul(
                                    bank, qTz[h % 2][:, ST * j + SC * c:ST * j + SC * c + SC],
                                    s1bz[:, DH * h:DH * h + DH],
                                    start=False, stop=True)
                            mm.then_inc(p.h, 1)
                        B.em("tensor", o1_mm)

                        vtok_b1 = v.n
                        vtok_o1ev = v.inc()

                        def o1_ev(vector, s=s, c=c, k=k_glob, tok=ptok_o1,
                                  tok_sh=xsh_tok[k_glob]):
                            vector.wait_ge(p.h, tok)
                            vector.wait_ge(d_xsh.h, tok_sh)
                            for h in range(16):
                                vector.tensor_scalar_mul(
                                    o1f[:, DH * h:DH * h + DH],
                                    ps_o[:, 512 * (h // 8) + DH * (h % 8):
                                         512 * (h // 8) + DH * (h % 8) + DH],
                                    gc_sb[:, 16 * c + h:16 * c + h + 1])
                            vector.tensor_copy(o1bf[:], o1f[:])
                            vector.tensor_tensor(e_bf[:], xsh[k % 2][:],
                                                 o1f[:], ALU.subtract).then_inc(v.h, 1)
                        B.em("vector", o1_ev)
                        emit_load_xsh(k_glob + 2)

                        if (s, c) in ((0, 0), (0, 1), (0, 3), (1, 0), (1, 3)):
                            emit_dump(f"o1f_{s}_{c}", o1f[:], (128, D), F32)
                            emit_dump(f"ebf_{s}_{c}", e_bf[:], (128, D), BF16)
                        if (s, c) in ((0, 1), (1, 0)):
                            emit_dump(f"s1bf_{s}_{c}", s1bz[:], (128, 1024), BF16)
                            emit_dump(f"s2bf_{s}_{c}", s2bz[:], (128, 1024), BF16)
                            emit_dump(f"vt_{s}_{c}", vt_bf[:], (128, D), BF16)

                        if attn_level < 4:
                            continue
                        for rnd in range(2):
                            ptok = p.inc()

                            def a2_mm(tensor, s=s, c=c, rnd=rnd,
                                      tok=v.n if rnd else vtok_b1):
                                tensor.wait_ge(v.h, tok)
                                for gg in range(2):
                                    g = 2 * rnd + gg
                                    for hh in range(4):
                                        h = 4 * g + hh
                                        j = h // 2
                                        sl = slice(ST * j + SC * c, ST * j + SC * c + SC)
                                        mm = tensor.matmul(
                                            ps_a[:, 512 * gg + SC * hh:512 * gg + SC * hh + SC],
                                            k2T[:, sl], q2Tz[h % 2][:, sl],
                                            start=True, stop=True)
                                mm.then_inc(p.h, 1)
                            B.em("tensor", a2_mm)
                            v.inc()

                            def a2_mask(vector, rnd=rnd, tok=ptok):
                                vector.wait_ge(p.h, tok)
                                for gg in range(2):
                                    g = 2 * rnd + gg
                                    ins = vector.tensor_tensor(
                                        b_bf[:, 4 * SC * g:4 * SC * g + 4 * SC],
                                        ps_a[:, 512 * gg:512 * gg + 512],
                                        mask_bf[:], ALU.mult)
                                ins.then_inc(v.h, 1)
                            B.em("vector", a2_mask)

                        if attn_level < 6:
                            continue
                        ptok_tr1 = p.inc()

                        def tr1(tensor, s=s, c=c, tok=vtok_o1ev):
                            tensor.wait_ge(v.h, tok)
                            for t in range(8):
                                mm = tensor.transpose(ps_t1[:, 128 * t:128 * t + 128],
                                                      o1bf[:, 128 * t:128 * t + 128], ident_bf[:])
                            mm.then_inc(p.h, 1)
                        B.em("tensor", tr1)

                        v.inc()

                        def tr1_ev(vector, s=s, c=c, tok=ptok_tr1):
                            vector.wait_ge(p.h, tok)
                            for t in range(8):
                                ins = vector.tensor_copy(
                                    oT[:, W2 * t + 1 + SC * c:W2 * t + 1 + SC * c + SC],
                                    ps_t1[:, 128 * t:128 * t + 128])
                            ins.then_inc(v.h, 1)
                        B.em("vector", tr1_ev)
                        gp.inc()

                        def tr1_gp(gpsimd, s=s, c=c, tok=v.n):
                            gpsimd.wait_ge(v.h, tok)
                            for t in range(8):
                                ins = gpsimd.tensor_add(
                                    oT[:, W2 * t + 1 + SC * c:W2 * t + 2 + SC * c],
                                    oT[:, W2 * t + 1 + SC * c:W2 * t + 2 + SC * c],
                                    o2carry[:, t:t + 1])
                            ins.then_inc(gp.h, 1)
                        B.em("gpsimd", tr1_gp)

                        ptok_o2 = p.inc()

                        def o2s_mm(tensor, s=s, c=c, k=k_glob, tok=v.n, tok_g=gp.n):
                            tensor.wait_ge(v.h, tok)
                            tensor.wait_ge(gp.h, tok_g)
                            for h in range(16):
                                j, p0_ = h // 2, 64 * (h % 2)
                                pr = h // 2
                                bank = ps_a[p0_:p0_ + 64,
                                            512 * (pr // 4) + SC * (pr % 4):
                                            512 * (pr // 4) + SC * (pr % 4) + SC]
                                tensor.matmul(bank, e_bf[:, DH * h:DH * h + DH],
                                              b_bf[:, SC * h:SC * h + SC],
                                              start=True, stop=False)
                                tensor.matmul(
                                    bank, s2bz[:, DH * h:DH * h + DH],
                                    q2Tz[h % 2][:, ST * j + SC * c:ST * j + SC * c + SC],
                                    start=False, stop=True)
                            for h in range(16):
                                p0_ = 64 * (h % 2)
                                tensor.matmul(ps_d1[p0_:p0_ + 64, DH * (h // 2):DH * (h // 2) + DH],
                                              knat[:, DH * h:DH * h + DH],
                                              vt_bf[:, DH * h:DH * h + DH],
                                              start=True, stop=True)
                                mm = tensor.matmul(ps_d2[p0_:p0_ + 64, DH * (h // 2):DH * (h // 2) + DH],
                                                   k2nat[:, DH * h:DH * h + DH],
                                                   e_bf[:, DH * h:DH * h + DH],
                                                   start=True, stop=True)
                            mm.then_inc(p.h, 1)
                        B.em("tensor", o2s_mm)

                        v.inc()

                        def o2_ev(vector, s=s, c=c, tok=ptok_o2):
                            vector.wait_ge(p.h, tok)
                            for pr in range(8):
                                reg = ps_a[:, 512 * (pr // 4) + SC * (pr % 4):
                                           512 * (pr // 4) + SC * (pr % 4) + SC]
                                vector.tensor_add(
                                    oT[:, W2 * pr + 2 + SC * c:W2 * pr + 1 + SC * c + SC],
                                    oT[:, W2 * pr + 2 + SC * c:W2 * pr + 1 + SC * c + SC],
                                    reg[:, 0:SC - 1])
                                vector.tensor_copy(o2carry[:, pr:pr + 1],
                                                   reg[:, SC - 1:SC])
                            vector.tensor_add(s1f[:], s1f[:], ps_d1[:])
                            ins = vector.tensor_add(s2f[:], s2f[:], ps_d2[:])
                            ins.then_inc(v.h, 1)
                        B.em("vector", o2_ev)
                        gp.inc()

                        def sbz_gp(gpsimd, tok=v.n):
                            gpsimd.wait_ge(v.h, tok)
                            for h in range(16):
                                pz = 64 * (h % 2)
                                gpsimd.tensor_copy(
                                    s1bz[pz:pz + 64, DH * h:DH * h + DH],
                                    s1f[pz:pz + 64, DH * (h // 2):DH * (h // 2) + DH])
                                ins = gpsimd.tensor_copy(
                                    s2bz[pz:pz + 64, DH * h:DH * h + DH],
                                    s2f[pz:pz + 64, DH * (h // 2):DH * (h // 2) + DH])
                            ins.then_inc(gp.h, 1)
                        B.em("gpsimd", sbz_gp)

                        if attn_level < 8:
                            continue
                        ptok_cp = p.inc()

                        def cp_mm(tensor, s=s, c=c, tok=v.n, tok_g=gp.n):
                            tensor.wait_ge(v.h, tok)
                            tensor.wait_ge(gp.h, tok_g)
                            for mh in range(2):
                                for t in range(8):
                                    tensor.matmul(
                                        ps_o[:, 512 * mh:512 * mh + 512],
                                        oT[:, W2 * t + 1 + SC * c:W2 * t + 1 + SC * c + SC],
                                        wcp_sb[:, D * t + 512 * mh:D * t + 512 * mh + 512],
                                        start=(t == 0), stop=False)
                                mm = tensor.matmul(ps_o[:, 512 * mh:512 * mh + 512],
                                                   ones_row_bf[:],
                                                   bcp_bf[0:1, 512 * mh:512 * mh + 512],
                                                   start=False, stop=True)
                            mm.then_inc(p.h, 1)
                        B.em("tensor", cp_mm)

                        if attn_level < 9:
                            continue
                        v.inc()

                        def ln_v1(vector, s=s, c=c, tok=ptok_cp):
                            vector.wait_ge(p.h, tok)
                            vector.tensor_tensor(z1[:], xn2[(s * NSC + c) % 2][:],
                                                 ps_o[:, 0:D], ALU.add)
                            vector.reduce_sum(stat[:, 0:1], z1[:], axis=AX.X)
                            vector.tensor_mul(o1f[:], z1[:], z1[:])
                            vector.reduce_sum(stat[:, 1:2], o1f[:],
                                              axis=AX.X).then_inc(v.h, 1)
                        B.em("vector", ln_v1)

                        if s == 0 and c == 0:
                            emit_dump("oT0", oT[:, 1:1 + SC], (128, SC), BF16)

                        gp.inc()

                        def ln_g1(gpsimd, tok=v.n):
                            gpsimd.wait_ge(v.h, tok)
                            gpsimd.tensor_mul(stat[:, 2:3], stat[:, 0:1],
                                              stat[:, 0:1]).then_inc(gp.h, 1)
                        B.em("gpsimd", ln_g1)
                        v.inc()

                        def ln_v1b(vector, tok=gp.n):
                            vector.wait_ge(gp.h, tok)
                            vector.scalar_tensor_tensor(stat[:, 3:4], stat[:, 1:2],
                                                        float(D), stat[:, 2:3],
                                                        ALU.mult, ALU.subtract).then_inc(v.h, 1)
                        B.em("vector", ln_v1b)
                        se.inc()

                        def ln_s1(scalar, tok=v.n):
                            scalar.wait_ge(v.h, tok)
                            scalar.activation(stat[:, 4:5], stat[:, 3:4], AF.Ln,
                                              scale=1.0 / (D * D),
                                              bias=eps_col[:, 0:1]).then_inc(se.h, 1)
                        B.em("scalar", ln_s1)
                        v.inc()

                        def ln_vc(vector, tok=se.n):
                            vector.wait_ge(se.h, tok)
                            vector.tensor_copy(stat[:, 6:7], stat[:, 4:5]).then_inc(v.h, 1)
                        B.em("vector", ln_vc)
                        se.inc()

                        def ln_s2(scalar, tok=v.n):
                            scalar.wait_ge(v.h, tok)
                            scalar.activation(stat[:, 5:6], stat[:, 6:7], AF.Exp,
                                              scale=-0.5).then_inc(se.h, 1)
                        B.em("scalar", ln_s2)
                        gp.inc()

                        def ln_g2(gpsimd, tok=se.n):
                            gpsimd.wait_ge(se.h, tok)
                            gpsimd.tensor_mul(stat[:, 7:8], stat[:, 0:1],
                                              rd_col[:, 0:1]).then_inc(gp.h, 1)
                        B.em("gpsimd", ln_g2)

                        emit_dump(f"stat_{s}_{c}", stat[:, 0:4], (128, 4), F32)
                        v.inc()

                        def ln_v2(vector, tok=gp.n, tok_sp=d_sp.n):
                            vector.wait_ge(gp.h, tok)
                            vector.wait_ge(d_sp.h, tok_sp)
                            vector.tensor_scalar(z1[:], z1[:], stat[:, 7:8], stat[:, 5:6],
                                                 ALU.subtract, ALU.mult)
                            vector.tensor_copy(x1bf_sb[:], z1[:]).then_inc(v.h, 1)
                        B.em("vector", ln_v2)

                        d_sp.inc(32)

                        def sp_dma(gpsimd, s=s, c=c, tok=v.n):
                            gpsimd.wait_ge(v.h, tok)
                            t0_ = s * ST + c * SC
                            gpsimd.dma_start(out=x1f_d[t0_:t0_ + SC, :],
                                             in_=z1[:]).then_inc(d_sp.h, 16)
                            gpsimd.dma_start(out=x1bf_d[t0_:t0_ + SC, :],
                                             in_=x1bf_sb[:]).then_inc(d_sp.h, 16)
                        B.em("gpsimd", sp_dma)
                        emit_load_xn(k_glob + 2)

                    emit_loads_lo(s + 1)

                if dbg_x1 is not None:
                    d_ld.inc(16)

                    def dbg(sync, tok=d_sp.n):
                        sync.wait_ge(d_sp.h, tok)
                        sync.dma_start(out=dbg_x1, in_=x1f_d).then_inc(d_ld.h, 16)
                    B.em("sync", dbg)

                B.flush()
            if stop_phase <= 5:
                return nc

        with contextlib.ExitStack() as eF:
            def sbufF(name, shape, dt=BF16):
                return eF.enter_context(nc.sbuf_tensor(name, list(shape), dt))

            x1T = sbufF("x1T", (128, 8 * ST))
            x1fx = sbufF("x1fx", (128, 4 * D), F32)
            ha = sbufF("ha", (128, 32 * ST))
            hb = sbufF("hb", (128, 32 * ST))
            xta = hb[:, 0:8 * ST]
            xtb = hb[:, 8 * ST:16 * ST]
            x1fs = ha[:].bitcast(F32)[:, 0:4 * D]
            pan = [sbufF(f"pan{i}", (128, 32 * ST)) for i in range(2)]
            pan2 = [sbufF(f"pan2_{i}", (128, 16 * ST)) for i in range(2)]
            sil = sbufF("sil", (128, ST))
            z2 = sbufF("z2", (128, D), F32)
            statf = sbufF("statf", (128, 8), F32)
            outf = sbufF("outf", (128, D), F32)
            pf1 = eF.enter_context(nc.psum_tensor("pf1", [128, 2048], F32))
            pf2 = eF.enter_context(nc.psum_tensor("pf2", [128, 2048], F32))
            d_pan = sem("d_pan")
            d_x1r = sem("d_x1r")

            def panel_load(jobs):
                gate_p = p.n
                gate_st = d_st.n
                d_pan.inc(16 * len(jobs))

                def f(sync, jobs=tuple(jobs), gate_p=gate_p, gate_st=gate_st):
                    sync.wait_ge(p.h, gate_p)
                    sync.wait_ge(d_st.h, gate_st)
                    for out_ap, in_ap in jobs:
                        sync.dma_start(out=out_ap, in_=in_ap).then_inc(d_pan.h, 16)
                B.em("sync", f)
                return d_pan.n

            def w12_jobs(og):
                return [(pan2[og % 2][:, 0:8 * ST].rearrange("q (k e) -> q k e", k=8),
                         wT_d["w1"][:, 512 * og:512 * og + 512].rearrange("(k q) e -> q k e", q=128)),
                        (pan2[og % 2][:, 8 * ST:16 * ST].rearrange("q (k e) -> q k e", k=8),
                         wT_d["w2"][:, 512 * og:512 * og + 512].rearrange("(k q) e -> q k e", q=128))]

            def h_jobs(lname, og):
                return [(pan[og % 2][:].rearrange("q (k e) -> q k e", k=32),
                         wT_d[lname][:, 512 * og:512 * og + 512].rearrange("(k q) e -> q k e", q=128))]

            def w3_jobs(mh):
                return [(pan[mh % 2][:, 0:32 * 512].rearrange("q (k e) -> q k e", k=32),
                         wT_d["w3"][:, 512 * mh:512 * mh + 512].rearrange("(k q) e -> q k e", q=128))]

            tok_sp_all = d_sp.n
            for f_i in range(NF):
                r0 = ST * f_i

                gate_v = v.n
                gate_p0 = p.n
                d_x1r.inc(16 * 18)

                def x1_loads(sync, f_i=f_i, r0=r0, gate_v=gate_v, tok_sp=tok_sp_all):
                    sync.wait_ge(v.h, gate_v)
                    sync.wait_ge(d_sp.h, tok_sp)
                    for j in range(8):
                        sync.dma_start(
                            out=xta[:, ST * j:ST * j + ST],
                            in_=x1bf_d[r0:r0 + ST, 128 * j:128 * j + 128],
                            transpose=True).then_inc(d_x1r.h, 16)
                        sync.dma_start(
                            out=xtb[:, ST * j:ST * j + ST],
                            in_=x1bf_d[HALF + r0:HALF + r0 + ST, 128 * j:128 * j + 128],
                            transpose=True).then_inc(d_x1r.h, 16)
                    sync.dma_start(
                        out=x1fx[:].rearrange("q (c e) -> q c e", c=4),
                        in_=x1f_d[r0:r0 + ST, :].rearrange("(c q) e -> q c e", q=128),
                    ).then_inc(d_x1r.h, 16)
                    sync.dma_start(
                        out=x1fs.rearrange("q (c e) -> q c e", c=4),
                        in_=x1f_d[HALF + r0:HALF + r0 + ST, :].rearrange("(c q) e -> q c e", q=128),
                    ).then_inc(d_x1r.h, 16)
                B.em("sync", x1_loads)
                tok_x1l = d_x1r.n

                v.inc()

                def x1_blend(vector, tok=tok_x1l, gate_p=gate_p0):
                    vector.wait_ge(d_x1r.h, tok)
                    vector.wait_ge(p.h, gate_p)
                    vector.tensor_scalar_mul(x1T[:], xta, hsel_sb[:, 1:2])
                    vector.scalar_tensor_tensor(x1T[:], xtb, hsel_sb[:, 0:1], x1T[:],
                                                ALU.mult, ALU.add)
                    vector.tensor_scalar_mul(x1fx[:], x1fx[:], hsel_sb[:, 1:2])
                    vector.scalar_tensor_tensor(x1fx[:], x1fs, hsel_sb[:, 0:1],
                                                x1fx[:], ALU.mult, ALU.add).then_inc(v.h, 1)
                B.em("vector", x1_blend)
                tok_x1 = v.n

                pan_tok = {0: panel_load(w12_jobs(0)), 1: panel_load(w12_jobs(1))}
                for og in range(8):
                    ptok_l1 = p.inc()

                    def l1_mm(tensor, og=og, tokp=pan_tok[og], tokx=tok_x1,
                              gate_v=v.n, gate_se=se.n):
                        tensor.wait_ge(d_pan.h, tokp)
                        tensor.wait_ge(v.h, max(tokx, gate_v) if og > 0 else tokx)
                        tensor.wait_ge(se.h, gate_se)
                        for lw in range(2):
                            base = 8 * ST * lw
                            pf = pf1 if lw == 0 else pf2
                            for oo in range(4):
                                reg = pf[:, 512 * oo:512 * oo + 512]
                                for k in range(8):
                                    mm = tensor.matmul(
                                        reg,
                                        pan2[og % 2][:, base + 512 * k + 128 * oo:
                                                     base + 512 * k + 128 * oo + 128],
                                        x1T[:, ST * k:ST * k + ST],
                                        start=(k == 0), stop=(k == 7))
                        mm.then_inc(p.h, 1)
                    B.em("tensor", l1_mm)

                    for oo in range(4):
                        se.inc()

                        def l1_sil(scalar, og=og, oo=oo, tok=ptok_l1, gate_v=v.n):
                            scalar.wait_ge(p.h, tok)
                            scalar.wait_ge(v.h, gate_v)
                            scalar.activation(sil[:], pf1[:, 512 * oo:512 * oo + 512],
                                              AF.Silu,
                                              scale=s1_sb[:, 4 * og + oo:4 * og + oo + 1]).then_inc(se.h, 1)
                        B.em("scalar", l1_sil)
                        stok = se.n
                        v.inc()

                        def l1_h(vector, og=og, oo=oo, tok=stok):
                            vector.wait_ge(se.h, tok)
                            ot = 4 * og + oo
                            vector.scalar_tensor_tensor(
                                ha[:, ST * ot:ST * ot + ST],
                                pf2[:, 512 * oo:512 * oo + 512],
                                s2_sb[:, ot:ot + 1], sil[:], ALU.mult, ALU.mult).then_inc(v.h, 1)
                        B.em("vector", l1_h)
                    if og + 2 < 8:
                        pan_tok[og + 2] = panel_load(w12_jobs(og + 2))

                next_tok = {("h1", 0): panel_load(h_jobs("h1", 0)),
                            ("h1", 1): panel_load(h_jobs("h1", 1))}
                for lname, nxt, src_h, dst_h, ssb in (("h1", "h2", ha, hb, sh1_sb),
                                                      ("h2", "w3", hb, ha, sh2_sb)):
                    pan_tok = {0: next_tok[(lname, 0)], 1: next_tok[(lname, 1)]}
                    for og in range(8):
                        ptok_l2 = p.inc()

                        def l2_mm(tensor, src_h=src_h, og=og, tokp=pan_tok[og],
                                  gate_v=v.n, gate_se=se.n):
                            tensor.wait_ge(d_pan.h, tokp)
                            tensor.wait_ge(v.h, gate_v)
                            tensor.wait_ge(se.h, gate_se)
                            for oo in range(4):
                                pf = pf1 if oo < 2 else pf2
                                reg = pf[:, 1024 * (oo % 2):1024 * (oo % 2) + 512]
                                for k in range(32):
                                    mm = tensor.matmul(
                                        reg,
                                        pan[og % 2][:, 512 * k + 128 * oo:512 * k + 128 * oo + 128],
                                        src_h[:, ST * k:ST * k + ST],
                                        start=(k == 0), stop=(k == 31))
                            mm.then_inc(p.h, 1)
                        B.em("tensor", l2_mm)

                        se.inc()

                        def l2_ev(scalar, dst_h=dst_h, ssb=ssb, og=og, tok=ptok_l2):
                            scalar.wait_ge(p.h, tok)
                            for oo in range(4):
                                ot = 4 * og + oo
                                pf = pf1 if oo < 2 else pf2
                                reg = pf[:, 1024 * (oo % 2):1024 * (oo % 2) + 512]
                                ins = scalar.activation(dst_h[:, ST * ot:ST * ot + ST], reg,
                                                        AF.Gelu, scale=ssb[:, ot:ot + 1])
                            ins.then_inc(se.h, 1)
                        B.em("scalar", l2_ev)
                        if og + 2 < 8:
                            pan_tok[og + 2] = panel_load(h_jobs(lname, og + 2))
                        elif og == 6:
                            jb = h_jobs(nxt, 0) if nxt != "w3" else w3_jobs(0)
                            next_tok[(nxt, 0)] = panel_load(jb)
                        elif og == 7:
                            jb = h_jobs(nxt, 1) if nxt != "w3" else w3_jobs(1)
                            next_tok[(nxt, 1)] = panel_load(jb)

                w3_tok = [next_tok[("w3", 0)], next_tok[("w3", 1)]]
                for c in range(4):
                    ptok_w3 = p.inc()

                    def w3_mm(tensor, c=c, tok1=w3_tok[1], gate_v=v.n, gate_se=se.n):
                        tensor.wait_ge(d_pan.h, tok1)
                        tensor.wait_ge(v.h, gate_v)
                        tensor.wait_ge(se.h, gate_se)
                        for mh in range(2):
                            reg = pf1[:, 512 * mh:512 * mh + 512]
                            for k in range(32):
                                mm = tensor.matmul(
                                    reg,
                                    ha[:, ST * k + SC * c:ST * k + SC * c + SC],
                                    pan[mh % 2][:, 512 * k:512 * k + 512],
                                    start=(k == 0), stop=(k == 31))
                        mm.then_inc(p.h, 1)
                    B.em("tensor", w3_mm)

                    v.inc()

                    def ln2_v1(vector, c=c, tok=ptok_w3, tok_out=d_out.n):
                        vector.wait_ge(p.h, tok)
                        vector.wait_ge(d_out.h, tok_out)
                        vector.tensor_tensor(z2[:], x1fx[:, D * c:D * c + D],
                                             pf1[:, 0:D], ALU.add)
                        vector.reduce_sum(statf[:, 0:1], z2[:], axis=AX.X)
                        vector.tensor_mul(outf[:], z2[:], z2[:])
                        vector.reduce_sum(statf[:, 1:2], outf[:],
                                          axis=AX.X).then_inc(v.h, 1)
                    B.em("vector", ln2_v1)
                    gp.inc()

                    def ln2_g1(gpsimd, tok=v.n):
                        gpsimd.wait_ge(v.h, tok)
                        gpsimd.tensor_mul(statf[:, 2:3], statf[:, 0:1],
                                          statf[:, 0:1]).then_inc(gp.h, 1)
                    B.em("gpsimd", ln2_g1)
                    v.inc()

                    def ln2_v1b(vector, tok=gp.n):
                        vector.wait_ge(gp.h, tok)
                        vector.scalar_tensor_tensor(statf[:, 3:4], statf[:, 1:2],
                                                    float(D), statf[:, 2:3],
                                                    ALU.mult, ALU.subtract).then_inc(v.h, 1)
                    B.em("vector", ln2_v1b)
                    se.inc()

                    def ln2_s1(scalar, tok=v.n):
                        scalar.wait_ge(v.h, tok)
                        scalar.activation(statf[:, 4:5], statf[:, 3:4], AF.Ln,
                                          scale=1.0 / (D * D),
                                          bias=eps_col[:, 0:1]).then_inc(se.h, 1)
                    B.em("scalar", ln2_s1)
                    v.inc()

                    def ln2_vc(vector, tok=se.n):
                        vector.wait_ge(se.h, tok)
                        vector.tensor_copy(statf[:, 6:7], statf[:, 4:5]).then_inc(v.h, 1)
                    B.em("vector", ln2_vc)
                    se.inc()

                    def ln2_s2(scalar, tok=v.n):
                        scalar.wait_ge(v.h, tok)
                        scalar.activation(statf[:, 5:6], statf[:, 6:7], AF.Exp,
                                          scale=-0.5).then_inc(se.h, 1)
                    B.em("scalar", ln2_s2)
                    gp.inc()

                    def ln2_g2(gpsimd, tok=se.n):
                        gpsimd.wait_ge(se.h, tok)
                        gpsimd.tensor_mul(statf[:, 7:8], statf[:, 0:1],
                                          rd_col[:, 0:1]).then_inc(gp.h, 1)
                    B.em("gpsimd", ln2_g2)
                    v.inc()

                    def ln2_v2(vector, tok=gp.n):
                        vector.wait_ge(gp.h, tok)
                        vector.tensor_scalar(outf[:], z2[:], statf[:, 7:8],
                                             statf[:, 5:6], ALU.subtract,
                                             ALU.mult).then_inc(v.h, 1)
                    B.em("vector", ln2_v2)

                    d_out.inc(16)

                    def out_dma(gpsimd, f_i=f_i, c=c, tok=v.n):
                        gpsimd.wait_ge(v.h, tok)
                        r = ST * f_i + SC * c
                        gpsimd.dma_start(out=out_d[r:r + SC, :],
                                         in_=outf[:]).then_inc(d_out.h, 16)
                    B.em("gpsimd", out_dma)

            def fin(gpsimd, tok=d_out.n):
                gpsimd.wait_ge(d_out.h, tok)
            B.em("gpsimd", fin)

            all_sems = [d_ld, d_st, d_out, d_pan, d_x1r] + dma_sems
            if dbg_sems[0] is not None:
                all_sems.append(dbg_sems[0])
            waits = [(s_.h, s_.n) for s_ in all_sems]

            def fin2(sync, waits=tuple(waits)):
                for h_, n_ in waits:
                    sync.wait_ge(h_, n_)
            B.em("sync", fin2)
            B.flush()

    return nc


_PROG_CACHE = {}


def kernel(**inputs):
    L = 8192
    HALF = L // 2
    x = np.ascontiguousarray(np.asarray(inputs["L0_attn_x"], dtype=np.float32))
    names = {
        "q1_w": "L0_attn_q1_w", "q1_b": "L0_attn_q1_b",
        "k1_w": "L0_attn_k1_w", "k1_b": "L0_attn_k1_b",
        "k2_w": "L0_attn_k2_w", "k2_b": "L0_attn_k2_b",
        "gw_w": "L0_attn_gw_w", "gw_b": "L0_attn_gw_b",
        "sw_w": "L0_attn_sw_w", "sw_b": "L0_attn_sw_b",
        "cp_w": "L0_attn_cp_w", "cp_b": "L0_attn_cp_b",
        "w1_w": "L0_ffn_w1_w", "w1_s": "L0_ffn_w1_s",
        "w2_w": "L0_ffn_w2_w", "w2_s": "L0_ffn_w2_s",
        "h1_w": "L0_ffn_h1_w", "h1_s": "L0_ffn_h1_s",
        "h2_w": "L0_ffn_h2_w", "h2_s": "L0_ffn_h2_s",
        "w3_w": "L0_ffn_w3_w", "w3_s": "L0_ffn_w3_s",
    }
    weights = {k: np.ascontiguousarray(np.asarray(inputs[srck], dtype=np.float32))
               for k, srck in names.items()}

    if "prog" not in _PROG_CACHE:
        _PROG_CACHE["prog"] = build_program(L=L)
    nc = _PROG_CACHE["prog"]

    in_maps = []
    for core in range(8):
        b, hf = core // 2, core % 2
        m = dict(weights)
        m["x"] = x[b]
        m["hsel"] = np.array([[float(hf), 1.0 - float(hf)]], dtype=np.float32)
        in_maps.append(m)
    res = run_bass_kernel_spmd(nc, in_maps, core_ids=list(range(8)))
    out = np.empty((4, L, 1024), dtype=np.float32)
    for core in range(8):
        b, hf = core // 2, core % 2
        out[b, hf * HALF:(hf + 1) * HALF, :] = res.results[core]["out"]
    return out
